# version 05
# speedup vs baseline: 1.4451x; 1.0701x over previous
import math
from contextlib import ExitStack
from types import SimpleNamespace

import numpy as np
import ml_dtypes

import concourse.bass as bass
import concourse.mybir as mybir
from concourse.bass_utils import run_bass_kernel_spmd

F32 = mybir.dt.float32
BF16 = mybir.dt.bfloat16
I32 = mybir.dt.int32
AF = mybir.ActivationFunctionType
ALU = mybir.AluOpType
AX = mybir.AxisListType

S = 2048
D = 1024
CH = D // 128
H = 16
HD = 64
E = 8
HID = 4096
NB = 32
MAXD = 128
ROPE_BASE = 10000.0
EPS = 1e-5
NEG = -1.0e9
SLINE = 4224
VP = 65
VW = 16 * H * VP


def _bucket_table():
    d = np.arange(S, dtype=np.int32)
    max_exact = NB // 2
    rp_f = np.maximum(d.astype(np.float32), np.float32(1.0))
    large = max_exact + (
        np.log(rp_f / np.float32(max_exact)).astype(np.float32)
        / np.float32(math.log(MAXD / max_exact)) * np.float32(NB - max_exact)
    ).astype(np.int32)
    large = np.minimum(large, NB - 1)
    return np.where(d < max_exact, d, large)


def _host_consts():
    c = {}
    flip = np.zeros((128, 128), dtype=ml_dtypes.bfloat16)
    for p in range(128):
        flip[p, 127 - p] = 1
    c["flip"] = flip
    c["ident32"] = np.eye(128, dtype=np.float32)
    c["identb"] = np.eye(128, dtype=ml_dtypes.bfloat16)
    c["ones1x128"] = np.ones((1, 128), dtype=ml_dtypes.bfloat16)
    c["ones128x1"] = np.ones((128, 1), dtype=ml_dtypes.bfloat16)
    hm = np.zeros((128, 128), dtype=ml_dtypes.bfloat16)
    for h in range(H):
        hm[(h % 2) * 64:(h % 2) * 64 + 64, (h // 2) * 16 + h] = 1
    c["hm16"] = hm
    bt = _bucket_table()
    oh = np.zeros((NB, SLINE), dtype=np.float32)
    for u in range(2175, 2175 + S):
        oh[bt[u - 2175], u] = 1.0
    c["onehotC"] = oh
    i = np.arange(0, HD, 2, dtype=np.float64) / HD
    inv_freq = 1.0 / (ROPE_BASE ** i)
    t = np.arange(S, dtype=np.float64)
    emb = np.concatenate([np.outer(t, inv_freq)] * 2, axis=1)
    cos = np.cos(emb).T.astype(np.float32)
    sin = np.sin(emb).T.astype(np.float32)
    sin[0:32, :] *= -1.0
    c["cosb"] = np.concatenate([cos, cos], axis=0)
    c["sinb"] = np.concatenate([sin, sin], axis=0)
    c["revw"] = np.tile((E - np.arange(E, dtype=np.float32)).reshape(1, E),
                        (128, 1))
    triu = np.zeros((128, 128), dtype=ml_dtypes.bfloat16)
    for tp in range(128):
        triu[tp, tp + 1:] = 1
    c["triuS"] = triu
    c["ones128x128"] = np.ones((128, 128), dtype=ml_dtypes.bfloat16)
    c["iotaE"] = np.tile(np.arange(E, dtype=np.float32).reshape(1, E),
                        (128, 1))
    tf = np.zeros((16, 16), dtype=np.float32)
    for tp in range(16):
        tf[tp, tp + 1:] = 1
    c["triuSf"] = tf
    return c


class Sem:
    def __init__(self, ctx, nc, name):
        self.h = ctx.enter_context(nc.semaphore(name))
        self.v = 0

    def inc(self, ins, n=1):
        ins.then_inc(self.h, n)
        self.v += n
        return ins

    def wait(self, eng, v=None):
        eng.wait_ge(self.h, self.v if v is None else v)


def dap(t, offset, dims):
    th = t.tensor if isinstance(t, bass.AP) else t
    return bass.AP(th, offset, [list(x) for x in dims])


class MK:
    def __init__(self, upto="all", dbg=()):
        self.upto = upto
        self.dbg = dbg
        self.dbg_out = {}
        self.nc = bass.Bass("TRN2", target_bir_lowering=False)

    def mkdbg(self, name, shape, dtype=F32):
        if name in self.dbg:
            t = self.nc.dram_tensor("dbg_" + name, list(shape), dtype,
                                    kind="ExternalOutput")
            self.dbg_out[name] = t
            return t
        return None

    def dump(self, name, sb_ap, shape, cast=False):
        t = self.mkdbg(name, shape)
        if t is None:
            return
        nc, s = self.nc, self.s
        self.barrier()
        with nc.Block() as blk:
            if cast:
                @blk.gpsimd
                def _(e):
                    s.stg.inc(e.dma_start(t.ap(), sb_ap), 16)
            else:
                @blk.sync
                def _(e):
                    s.st.inc(e.dma_start(t.ap(), sb_ap), 16)

    def dump_dram(self, name, dt_, shape, dtype=F32):
        t = self.mkdbg(name, shape, dtype)
        if t is None:
            return
        nc, s = self.nc, self.s
        self.barrier()
        n = int(np.prod(shape))
        rows = n // 2048
        with ExitStack() as ctx:
            buf = ctx.enter_context(nc.sbuf_tensor("dbgbuf" + name,
                                                   [128, 2048], dt_.dtype))
            with nc.Block() as blk:
                @blk.sync
                def _(e):
                    for r in range(0, rows, 128):
                        nr = min(128, rows - r)
                        v0 = s.sa.v
                        s.sa.inc(e.dma_start(
                            buf[0:nr, :],
                            dap(dt_, r * 2048, [[2048, nr], [1, 2048]])), 16)
                        e.wait_ge(s.sa.h, v0 + 16)
                        s.st.inc(e.dma_start(
                            dap(t, r * 2048, [[2048, nr], [1, 2048]]),
                            buf[0:nr, :]), 16)
                        e.wait_ge(s.st.h, s.st.v)

    def barrier(self):
        nc, s = self.nc, self.s
        with nc.Block() as blk:
            @blk.tensor
            def _(e):
                e.drain()
                s.pe.inc(e.nop())
            @blk.scalar
            def _(e):
                e.drain()
                s.act.inc(e.nop())
            @blk.vector
            def _(e):
                e.drain()
                s.dve.inc(e.nop())
            @blk.sync
            def _(e):
                s.sp.wait(e)
                s.spb.inc(e.nop())
            @blk.gpsimd
            def _(e):
                s.gp.wait(e)
                s.gpb.inc(e.nop())
        with nc.Block() as blk:
            @blk.tensor
            def _(e):
                for x in (s.act, s.dve, s.spb, s.gpb):
                    x.wait(e)
            @blk.scalar
            def _(e):
                for x in (s.pe, s.dve, s.spb, s.gpb):
                    x.wait(e)
            @blk.vector
            def _(e):
                for x in (s.pe, s.act, s.spb, s.gpb):
                    x.wait(e)
            @blk.sync
            def _(e):
                for x in (s.pe, s.act, s.dve, s.gpb):
                    x.wait(e)
            @blk.gpsimd
            def _(e):
                for x in (s.pe, s.act, s.dve, s.spb):
                    x.wait(e)

    def build(self):
        nc = self.nc
        inp = SimpleNamespace()
        inp.x = nc.dram_tensor("x", [S, D], F32, kind="ExternalInput")
        inp.wq = nc.dram_tensor("wq", [D, D], F32, kind="ExternalInput")
        inp.wk = nc.dram_tensor("wk", [D, D], F32, kind="ExternalInput")
        inp.wv = nc.dram_tensor("wv", [D, D], F32, kind="ExternalInput")
        inp.wo = nc.dram_tensor("wo", [D, D], F32, kind="ExternalInput")
        inp.rel = nc.dram_tensor("rel", [NB, H], F32, kind="ExternalInput")
        inp.ln1w = nc.dram_tensor("ln1w", [1, D], F32, kind="ExternalInput")
        inp.ln1b = nc.dram_tensor("ln1b", [1, D], F32, kind="ExternalInput")
        inp.ln2w = nc.dram_tensor("ln2w", [1, D], F32, kind="ExternalInput")
        inp.ln2b = nc.dram_tensor("ln2b", [1, D], F32, kind="ExternalInput")
        inp.wqkv = nc.dram_tensor("wqkv", [D, 3 * D], F32, kind="ExternalInput")
        inp.wproj = nc.dram_tensor("wproj", [D, D], F32, kind="ExternalInput")
        inp.gatew = nc.dram_tensor("gatew", [D, E], F32, kind="ExternalInput")
        inp.gateb = nc.dram_tensor("gateb", [1, E], F32, kind="ExternalInput")
        inp.w1 = nc.dram_tensor("w1", [E, D, HID], F32, kind="ExternalInput")
        inp.w2 = nc.dram_tensor("w2", [E, HID, D], F32, kind="ExternalInput")
        self.inp = inp
        self.out = nc.dram_tensor("out", [S, D], F32, kind="ExternalOutput")

        scr = SimpleNamespace()
        scr.xfm16 = nc.dram_tensor("xfm16", [CH, 128, S], BF16, kind="Internal")
        scr.xfm32 = nc.dram_tensor("xfm32", [CH, 128, S], F32, kind="Internal")
        scr.x1d = nc.dram_tensor("x1d", [CH, 128, S], F32, kind="Internal")
        scr.rd = nc.dram_tensor("rd", [CH, 128, S], F32, kind="Internal")
        scr.xln2d = nc.dram_tensor("xln2d", [CH, 128, S], BF16, kind="Internal")
        scr.slined = nc.dram_tensor("slined", [H, SLINE], BF16, kind="Internal")
        scr.ohd = nc.dram_tensor("ohd", [E, S], BF16, kind="Internal")
        scr.xln2t = nc.dram_tensor("xln2t", [S, D], BF16, kind="Internal")
        scr.recipd = nc.dram_tensor("recipd", [H, S], F32, kind="Internal")
        scr.xcomp = nc.dram_tensor("xcomp", [4352, D], BF16, kind="Internal")
        scr.ycomp = nc.dram_tensor("ycomp", [4352, D], F32, kind="Internal")
        scr.pexpd = nc.dram_tensor("pexpd", [4352, 1], F32, kind="Internal")
        self.scr = scr

        hc = _host_consts()
        self.cst = {k: nc.inline_tensor(v, name="c_" + k)
                    for k, v in hc.items()}

        s = SimpleNamespace()
        self.s = s
        with ExitStack() as ctx:
            for name in ("ld", "gp", "pe", "act", "dve", "sp", "st",
                         "sa", "sb", "sc", "sd", "se", "sf", "sg", "sh",
                         "si", "sj", "sk", "sl", "sm", "sn", "spb", "gpb",
                     "stg"):
                setattr(s, name, Sem(ctx, nc, "sem_" + name))
            sb = ctx.enter_context
            t = SimpleNamespace()
            self.t = t
            t.flip = sb(nc.sbuf_tensor("flip", [128, 128], BF16))
            t.ident = sb(nc.sbuf_tensor("ident", [128, 128], F32))
            t.identb = sb(nc.sbuf_tensor("identb", [128, 128], BF16))
            t.ones1x128 = sb(nc.sbuf_tensor("ones1x128", [1, 128], BF16))
            t.ones1f = sb(nc.sbuf_tensor("ones1f", [1, 128], F32))
            t.ones128 = sb(nc.sbuf_tensor("ones128", [128, 1], BF16))
            t.hm16 = sb(nc.sbuf_tensor("hm16", [128, 128], BF16))
            t.bmax16 = sb(nc.sbuf_tensor("bmax16", [H, 1], F32))
            t.negb = sb(nc.sbuf_tensor("negb", [128, H], F32))
            t.lnw = sb(nc.sbuf_tensor("lnw", [128, 4 * CH], F32))
            t.gws = sb(nc.sbuf_tensor("gws", [128, CH * E], BF16))
            t.gbb = sb(nc.sbuf_tensor("gbb", [128, E], F32))
            t.revw = sb(nc.sbuf_tensor("revw", [128, E], F32))
            t.relsb = sb(nc.sbuf_tensor("relsb", [NB, H], F32))
            t.epsc = sb(nc.sbuf_tensor("epsc", [128, 2], F32))
            t.triuS = sb(nc.sbuf_tensor("triuS", [128, 128], BF16))
            t.allon = sb(nc.sbuf_tensor("allon", [128, 128], BF16))
            t.iotaE = sb(nc.sbuf_tensor("iotaE", [128, E], F32))
            t.sloti = sb(nc.sbuf_tensor("sloti", [128, 16], I32))
            t.triuSf = sb(nc.sbuf_tensor("triuSf", [16, 16], F32))

            self.phase_setup()
            if self.upto == "setup":
                return self.finish()
            self.phase_load_x()
            if self.upto == "loadx":
                return self.finish()

            abctx = ctx.enter_context(ExitStack())
            t.ctx_bf = abctx.enter_context(
                nc.sbuf_tensor("ctxAB", [128, CH * S], BF16))
            with ExitStack() as actx:
                a = actx.enter_context
                t.q_bf = a(nc.sbuf_tensor("qA", [128, CH * S], BF16))
                t.k_bf = a(nc.sbuf_tensor("kA", [128, CH * S], BF16))
                t.v_tok = a(nc.sbuf_tensor("vA", [128, VW], BF16))
                self.phase_proj(block="A")
                self.phase_bounds(causal=True)
                if self.upto == "qkvA":
                    self.dbg_qkv()
                    return self.finish()
                self.phase_attn(causal=True)
                if self.upto == "attnA":
                    self.dbg_ctx()
                    return self.finish()
            t.xln1_bf = abctx.enter_context(
                nc.sbuf_tensor("xln1", [128, CH * S], BF16, side="right"))
            self.phase_out_ln(block="A")
            if self.upto == "x1":
                self.dump_dram("x1", scr.x1d, [CH * 128, S])
                self.dump("xln1", t.xln1_bf[:, :], [128, CH * S], cast=True)
                return self.finish()

            with ExitStack() as bctx:
                b = bctx.enter_context
                t.q_bf = b(nc.sbuf_tensor("qB", [128, CH * S], BF16))
                t.k_bf = b(nc.sbuf_tensor("kB", [128, CH * S], BF16))
                t.v_tok = b(nc.sbuf_tensor("vB", [128, VW], BF16))
                self.phase_proj(block="B")
                self.phase_bounds(causal=False)
                if self.upto == "qkvB":
                    self.dbg_qkv()
                    return self.finish()
                self.phase_attn(causal=False)
                if self.upto == "attnB":
                    self.dbg_ctx()
                    return self.finish()
            self.phase_out_ln(block="B")
            abctx.__exit__(None, None, None)
            if self.upto == "route":
                self.dump("sloti", self.t.sloti[:, :], [128, 16], cast=True)
                self.dump_dram("xcq", self.scr.xcomp, [4352, D], BF16)
                return self.finish()
            if self.upto == "xln2":
                self.dump("gws", t.gws[:, :], [128, CH * E], cast=True)
                self.dump("gbb", t.gbb[:, :], [128, E])
                self.dump_dram("rr", scr.rd, [CH * 128, S])
                self.dump_dram("xln2", scr.xln2d, [CH * 128, S], BF16)
                self.dump_dram("ohdd", scr.ohd, [E, S], BF16)
                return self.finish()

            self.phase_moe_final()
            return self.finish()

    def finish(self):
        nc, s = self.nc, self.s
        self.barrier()
        with nc.Block() as blk:
            @blk.sync
            def _(e):
                s.st.wait(e)
            @blk.gpsimd
            def _(e):
                s.stg.wait(e)
        return self.nc, self.dbg_out

    def phase_setup(self):
        nc, s, t, inp, scr, cst = (self.nc, self.s, self.t, self.inp,
                                   self.scr, self.cst)
        with nc.Block() as blk:
            @blk.sync
            def _(e):
                for dst, src in (
                    (t.flip, cst["flip"]), (t.ident, cst["ident32"]),
                    (t.identb, cst["identb"]),
                    (t.ones1x128, cst["ones1x128"]),
                    (t.ones128, cst["ones128x1"]), (t.hm16, cst["hm16"]),
                    (t.revw, cst["revw"]), (t.relsb, inp.rel),
                    (t.triuS, cst["triuS"]), (t.allon, cst["ones128x128"]),
                    (t.iotaE, cst["iotaE"]), (t.triuSf, cst["triuSf"]),
                ):
                    s.ld.inc(e.dma_start(dst[:, :], src.ap()), 16)
                for g, tt in enumerate((inp.ln1w, inp.ln1b, inp.ln2w,
                                        inp.ln2b)):
                    s.ld.inc(e.dma_start(
                        dap(t.lnw, g * CH, [[4 * CH, 128], [1, CH]]),
                        dap(tt, 0, [[1, 128], [128, CH]]),
                        allow_slow_non_contiguous=True), 16)

            @blk.gpsimd
            def _(e):
                for c in range(CH):
                    s.gp.inc(e.dma_start(
                        dap(t.gws, c * E, [[CH * E, 128], [1, E]]),
                        dap(inp.gatew, c * 128 * E, [[E, 128], [1, E]])), 16)

            @blk.vector
            def _(e):
                s.dve.inc(e.memset(t.ones1f[:, :], 1.0))
                s.dve.inc(e.memset(t.epsc[:, 0:1], EPS))
                s.dve.inc(e.memset(t.epsc[:, 1:2], 1e-30))

        with ExitStack() as zctx:
            zbuf = zctx.enter_context(nc.sbuf_tensor("zbuf", [128, 8192],
                                                     BF16))
            with nc.Block() as blk:
                @blk.vector
                def _(e):
                    s.dve.inc(e.memset(zbuf[:, :], 0.0))

                @blk.sync
                def _(e):
                    s.dve.wait(e)
                    for i in range(4):
                        s.sp.inc(e.dma_start(
                            dap(scr.xcomp, i * 128 * 8192,
                                [[8192, 128], [1, 8192]]),
                            zbuf[:, :]), 16)
                    s.sp.inc(e.dma_start(
                        dap(scr.xcomp, 4 * 128 * 8192,
                            [[8192, 32], [1, 8192]]),
                        zbuf[0:32, :]), 16)
                    s.sp.wait(e)

                @blk.gpsimd
                def _(e):
                    s.dve.wait(e)
                    s.gp.inc(e.dma_start(
                        dap(scr.pexpd, 0, [[1, 1], [1, 4352]]),
                        zbuf[0:1, 0:4352]), 16)
                    s.gp.wait(e)

        with ExitStack() as ctx:
            ohc = ctx.enter_context(nc.sbuf_tensor("ohc", [NB, SLINE], F32))
            slsb = ctx.enter_context(nc.sbuf_tensor("slsb", [H, SLINE], F32))
            gbrow = ctx.enter_context(nc.sbuf_tensor("gbrow", [1, E], F32))
            pst = ctx.enter_context(nc.psum_tensor("pst", [H, 1408], F32))
            pti = ctx.enter_context(nc.psum_tensor("pti", [128, 16], F32))
            ld0, pe0, act0, dve0 = s.ld.v, s.pe.v, s.act.v, s.dve.v
            with nc.Block() as blk:
                @blk.sync
                def _(e):
                    s.ld.inc(e.dma_start(ohc[:, :], cst["onehotC"].ap()), 16)
                    s.ld.inc(e.dma_start(gbrow[:, :], inp.gateb.ap()), 16)

                @blk.tensor
                def _(e):
                    e.wait_ge(s.ld.h, ld0 + 32)
                    e.wait_ge(s.dve.h, dve0)
                    for r in range(3):
                        if r:
                            e.wait_ge(s.act.h, act0 + r)
                        for cc in range(11):
                            u = r * 11 + cc
                            ins = e.matmul(pst[:, cc * 128:(cc + 1) * 128],
                                           t.relsb[:, :],
                                           ohc[:, u * 128:(u + 1) * 128],
                                           start=True, stop=True)
                        s.pe.inc(ins)
                    s.pe.inc(e.matmul(pti[:, 0:E], t.ones1f[:, :],
                                      gbrow[:, :], start=True, stop=True))

                @blk.scalar
                def _(e):
                    for r in range(3):
                        e.wait_ge(s.pe.h, pe0 + r + 1)
                        s.act.inc(e.copy(slsb[:, r * 1408:(r + 1) * 1408],
                                         pst[:, :]))
                    e.wait_ge(s.pe.h, pe0 + 4)
                    s.act.inc(e.copy(t.gbb[:, :], pti[:, 0:E]))

                @blk.vector
                def _(e):
                    e.wait_ge(s.act.h, act0 + 3)
                    d0 = s.dve.v
                    s.dve.inc(e.memset(slsb[:, 0:2175], NEG))
                    e.wait_ge(s.dve.h, d0 + 1)
                    s.dve.inc(e.reduce_max(t.bmax16[:, 0:1], slsb[:, :],
                                           axis=AX.X))
                    e.wait_ge(s.dve.h, d0 + 2)
                    s.dve.inc(e.tensor_scalar_max(t.bmax16[:, 0:1],
                                                  t.bmax16[:, 0:1], 0.0))

                @blk.gpsimd
                def _(e):
                    s.dve.wait(e)
                    s.gp.inc(e.dma_start(scr.slined.ap(), slsb[:, :]), 16)
        self.barrier()

    def phase_load_x(self):
        nc, s, t, inp, scr = self.nc, self.s, self.t, self.inp, self.scr
        with ExitStack() as ctx:
            xt = ctx.enter_context(nc.sbuf_tensor("xt", [128, 2 * D], F32))
            xbf = ctx.enter_context(nc.sbuf_tensor("xbf", [128, 2048], BF16))
            xf32 = ctx.enter_context(nc.sbuf_tensor("xf32", [128, 2048], F32))
            pt = ctx.enter_context(nc.psum_tensor("ptx", [128, 2048], F32))
            sA, sB, sC = s.sa, s.sb, s.sc
            sD = s.gp
            a0, b0, c0, d0 = sA.v, sB.v, sC.v, sD.v
            with nc.Block() as blk:
                @blk.sync
                def _(e):
                    for tt in range(16):
                        if tt >= 2:
                            sB.wait(e, b0 + tt - 1)
                        sA.inc(e.dma_start(
                            xt[:, (tt % 2) * D:(tt % 2) * D + D],
                            dap(inp.x, tt * 128 * D, [[D, 128], [1, D]])), 16)

                @blk.tensor
                def _(e):
                    for tt in range(16):
                        sA.wait(e, a0 + 16 * (tt + 1))
                        if tt >= 2:
                            sC.wait(e, c0 + 2 * (tt - 1))
                        for c in range(CH):
                            ins = e.transpose(
                                pt[:, (tt % 2) * 1024 + c * 128:
                                   (tt % 2) * 1024 + (c + 1) * 128],
                                xt[:, (tt % 2) * D + c * 128:
                                   (tt % 2) * D + (c + 1) * 128],
                                t.ident[:, :])
                        sB.inc(ins)

                @blk.scalar
                def _(e):
                    for tt in range(16):
                        sB.wait(e, b0 + tt + 1)
                        if tt >= 2:
                            sD.wait(e, d0 + 32 * (tt - 1))
                        sC.inc(e.copy(
                            xbf[:, (tt % 2) * 1024:(tt % 2) * 1024 + 1024],
                            pt[:, (tt % 2) * 1024:(tt % 2) * 1024 + 1024]))
                        sC.inc(e.copy(
                            xf32[:, (tt % 2) * 1024:(tt % 2) * 1024 + 1024],
                            pt[:, (tt % 2) * 1024:(tt % 2) * 1024 + 1024]))

                @blk.gpsimd
                def _(e):
                    for tt in range(16):
                        sC.wait(e, c0 + 2 * (tt + 1))
                        sD.inc(e.dma_start(
                            dap(scr.xfm16, tt * 128,
                                [[S, 128], [128 * S, CH], [1, 128]]),
                            dap(xbf, (tt % 2) * 1024,
                                [[2048, 128], [128, CH], [1, 128]])), 16)
                        sD.inc(e.dma_start(
                            dap(scr.xfm32, tt * 128,
                                [[S, 128], [128 * S, CH], [1, 128]]),
                            dap(xf32, (tt % 2) * 1024,
                                [[2048, 128], [128, CH], [1, 128]])), 16)
        self.barrier()

    def _wslice_ap(self, block, which, g):
        inp = self.inp
        if block == "A":
            w = {"q": inp.wq, "k": inp.wk}[which]
            base, rowlen = 0, D
        else:
            w = inp.wqkv
            base = {"q": 0, "k": D}[which]
            rowlen = 3 * D
        return dap(w, base + g * 256,
                   [[rowlen, 128], [128 * rowlen, CH], [1, 256]])

    def phase_proj(self, block):
        nc, s, t, scr, cst = self.nc, self.s, self.t, self.scr, self.cst
        rope = (block == "A")
        with ExitStack() as qkctx:
            ctx = qkctx
            if rope:
                t.cosb = ctx.enter_context(nc.sbuf_tensor("cosb", [128, S],
                                                          F32))
                t.sinb = ctx.enter_context(nc.sbuf_tensor("sinb", [128, S],
                                                          F32))
            wsl = [ctx.enter_context(nc.sbuf_tensor(f"wsl{block}{i}",
                                                    [128, CH * 256], BF16))
                   for i in range(2)]
            if block == "A":
                xc = [ctx.enter_context(nc.sbuf_tensor(f"xc{i}", [128, S],
                                                       BF16))
                      for i in range(2)]
            qraw = ctx.enter_context(nc.sbuf_tensor("qraw" + block, [128, S], F32))
            qx32 = ctx.enter_context(nc.sbuf_tensor("qx32" + block, [128, S], F32))
            qc_ = ctx.enter_context(nc.sbuf_tensor("qc_" + block, [128, S], F32))
            ps = [ctx.enter_context(nc.psum_tensor(f"psp{block}{i}", [128, S], F32))
                  for i in range(2)]

            sW, sX, sM, sR, sSh, sRd = s.gp, s.sb, s.sc, s.sd, s.se, s.sf
            w0, x0, m0, r0, sh0, rd0 = sW.v, sX.v, sM.v, sR.v, sSh.v, sRd.v
            projs = [("q", g) for g in range(4)] + [("k", g) for g in range(4)]

            with nc.Block() as blk:
                if rope:
                    @blk.sync
                    def _(e):
                        s.ld.inc(e.dma_start(t.cosb[:, :], cst["cosb"].ap()),
                                 16)
                        s.ld.inc(e.dma_start(t.sinb[:, :], cst["sinb"].ap()),
                                 16)

                @blk.gpsimd
                def _(e):
                    for i, (which, g) in enumerate(projs):
                        if i >= 2:
                            sM.wait(e, m0 + (i - 1) * 8)
                        sW.inc(e.dma_start(wsl[i % 2][:, :],
                                           self._wslice_ap(block, which, g)),
                               16)

                if block == "A":
                    @blk.sync
                    def _(e):
                        for i in range(8):
                            for c in range(CH):
                                n = i * 8 + c
                                if n >= 2:
                                    sM.wait(e, m0 + n - 1)
                                sX.inc(e.dma_start(
                                    xc[n % 2][:, :],
                                    dap(scr.xfm16, c * 128 * S,
                                        [[S, 128], [1, S]])), 16)
                            for pair in range(2):
                                j = i * 2 + pair
                                sR.wait(e, r0 + j + 1)
                                for gb in range(4):
                                    srcp = (gb ^ 1) * 32
                                    sSh.inc(e.dma_start(
                                        qx32[gb * 32:(gb + 1) * 32, :],
                                        qraw[srcp:srcp + 32, :]), 16)

                @blk.tensor
                def _(e):
                    for i, (which, g) in enumerate(projs):
                        sW.wait(e, w0 + 16 * (i + 1))
                        if i >= 1:
                            if rope:
                                sR.wait(e, r0 + i * 2)
                            else:
                                sRd.wait(e, rd0 + i * 2)
                        for c in range(CH):
                            n = i * 8 + c
                            if block == "A":
                                sX.wait(e, x0 + 16 * (n + 1))
                            for pair in range(2):
                                for q4 in range(4):
                                    if block == "A":
                                        rhs = xc[n % 2][:, q4 * 512:
                                                        (q4 + 1) * 512]
                                    else:
                                        rhs = t.xln1_bf[
                                            :, c * S + q4 * 512:
                                            c * S + (q4 + 1) * 512]
                                    ins = e.matmul(
                                        ps[pair][:, q4 * 512:(q4 + 1) * 512],
                                        wsl[i % 2][:, c * 256 + pair * 128:
                                                   c * 256 + (pair + 1) * 128],
                                        rhs, start=(c == 0),
                                        stop=(c == CH - 1))
                            sM.inc(ins)

                if rope:
                    @blk.scalar
                    def _(e):
                        e.wait_ge(s.ld.h, s.ld.v)
                        for i in range(8):
                            for pair in range(2):
                                j = i * 2 + pair
                                sM.wait(e, m0 + (i + 1) * 8)
                                if j >= 1:
                                    sRd.wait(e, rd0 + j)
                                sR.inc(e.copy(qraw[:, :], ps[pair][:, :]))

                    @blk.vector
                    def _(e):
                        for i, (which, g) in enumerate(projs):
                            for pair in range(2):
                                j = i * 2 + pair
                                sR.wait(e, r0 + j + 1)
                                dv = s.dve.v
                                s.dve.inc(e.tensor_tensor(
                                    qc_[:, :], qraw[:, :], t.cosb[:, :],
                                    ALU.mult))
                                sSh.wait(e, sh0 + 64 * (j + 1))
                                s.dve.inc(e.tensor_tensor(
                                    qx32[:, :], qx32[:, :], t.sinb[:, :],
                                    ALU.mult))
                                e.wait_ge(s.dve.h, dv + 2)
                                dst = t.q_bf if which == "q" else t.k_bf
                                co = (2 * g + pair) * S
                                if which == "q":
                                    s.dve.inc(e.tensor_tensor(
                                        qc_[:, :], qc_[:, :], qx32[:, :],
                                        ALU.add))
                                    e.wait_ge(s.dve.h, dv + 3)
                                    sRd.inc(e.tensor_scalar_mul(
                                        dst[:, co:co + S], qc_[:, :], 0.125))
                                else:
                                    sRd.inc(e.tensor_tensor(
                                        dst[:, co:co + S], qc_[:, :],
                                        qx32[:, :], ALU.add))
                else:
                    @blk.scalar
                    def _(e):
                        for i, (which, g) in enumerate(projs):
                            for pair in range(2):
                                sM.wait(e, m0 + (i + 1) * 8)
                                co = (2 * g + pair) * S
                                if which == "k":
                                    sRd.inc(e.copy(t.k_bf[:, co:co + S],
                                                   ps[pair][:, :]))
                                else:
                                    sRd.inc(e.mul(t.q_bf[:, co:co + S],
                                                  ps[pair][:, :], 0.125))

        self.barrier()
        with ExitStack() as vctx:
            self._proj_v(block, vctx)
        self.barrier()

    def _proj_v(self, block, ctx):
        nc, s, t, scr = self.nc, self.s, self.t, self.scr
        wv = ctx.enter_context(nc.sbuf_tensor("wvsb" + block, [128, CH * D], BF16))
        if block == "A":
            xtt = [ctx.enter_context(nc.sbuf_tensor(f"xtt{i}", [128, D],
                                                    BF16)) for i in range(2)]
        psv = [ctx.enter_context(nc.psum_tensor(f"psv{block}{i}", [128, D], F32))
               for i in range(2)]
        sV, sXv, sEv = s.sg, s.sh, s.si
        sWl = s.gp
        v0, x0, e0, wl0 = sV.v, sXv.v, sEv.v, sWl.v
        with nc.Block() as blk:
            @blk.gpsimd
            def _(e):
                if block == "A":
                    ap = dap(self.inp.wv, 0, [[D, 128], [128 * D, CH], [1, D]])
                else:
                    ap = dap(self.inp.wqkv, 2 * D,
                             [[3 * D, 128], [128 * 3 * D, CH], [1, D]])
                sWl.inc(e.dma_start(wv[:, :], ap), 16)

            if block == "A":
                @blk.sync
                def _(e):
                    for tt in range(16):
                        if tt >= 2:
                            sV.wait(e, v0 + tt - 1)
                        sXv.inc(e.dma_start(
                            xtt[tt % 2][:, :],
                            dap(scr.xfm16, tt * 128,
                                [[S, 128], [128 * S, CH], [1, 128]])), 16)

            @blk.tensor
            def _(e):
                sWl.wait(e, wl0 + 16)
                for tt in range(16):
                    if block == "A":
                        sXv.wait(e, x0 + 16 * (tt + 1))
                    if tt >= 2:
                        sEv.wait(e, e0 + tt - 1)
                    for c in range(CH):
                        for nh in range(2):
                            if block == "A":
                                lhs = xtt[tt % 2][:, c * 128:(c + 1) * 128]
                            else:
                                lhs = t.xln1_bf[:, c * S + tt * 128:
                                                c * S + (tt + 1) * 128]
                            ins = e.matmul(
                                psv[tt % 2][:, nh * 512:(nh + 1) * 512],
                                lhs,
                                wv[:, c * D + nh * 512:c * D + (nh + 1) * 512],
                                start=(c == 0), stop=(c == CH - 1))
                    sV.inc(ins)

            @blk.scalar
            def _(e):
                for tt in range(16):
                    sV.wait(e, v0 + tt + 1)
                    sEv.inc(e.copy(
                        dap(t.v_tok, tt * H * VP,
                            [[VW, 128], [VP, H], [1, HD]]),
                        psv[tt % 2][:, :]))

            @blk.vector
            def _(e):
                s.dve.inc(e.memset(
                    dap(t.v_tok, HD, [[VW, 128], [VP, 16 * H], [1, 1]]), 1.0))

    def phase_bounds(self, causal):
        nc, s, t = self.nc, self.s, self.t
        with ExitStack() as ctx:
            sq = [ctx.enter_context(nc.sbuf_tensor(f"sqb{causal}{i}", [128, S], BF16))
                  for i in range(2)]
            mq2 = ctx.enter_context(nc.sbuf_tensor("mq2" + str(causal), [H, 1], F32))
            mk2 = ctx.enter_context(nc.sbuf_tensor("mk2" + str(causal), [H, 1], F32))
            brow = ctx.enter_context(nc.sbuf_tensor("brow" + str(causal), [1, H], F32))
            with ExitStack() as pctx:
                pn = [pctx.enter_context(nc.psum_tensor(f"pn{causal}{i}", [H, S],
                                                        F32))
                      for i in range(2)]
                sQ = s.sa
                q0, pe0, d0 = sQ.v, s.pe.v, s.dve.v
                with nc.Block() as blk:
                    @blk.scalar
                    def _(e):
                        for n in range(16):
                            if n >= 2:
                                e.wait_ge(s.pe.h, pe0 + n - 1)
                            src = t.q_bf if n < 8 else t.k_bf
                            c = n % 8
                            sQ.inc(e.activation(
                                sq[n % 2][:, :],
                                src[:, c * S:(c + 1) * S], AF.Square))

                    @blk.tensor
                    def _(e):
                        for n in range(16):
                            j, c = n // 8, n % 8
                            sQ.wait(e, q0 + n + 1)
                            for q4 in range(4):
                                ins = e.matmul(
                                    pn[j][:, q4 * 512:(q4 + 1) * 512],
                                    t.hm16[:, c * 16:(c + 1) * 16],
                                    sq[n % 2][:, q4 * 512:(q4 + 1) * 512],
                                    start=(c == 0), stop=(c == CH - 1))
                            s.pe.inc(ins)

                    @blk.vector
                    def _(e):
                        e.wait_ge(s.pe.h, pe0 + 16)
                        s.dve.inc(e.reduce_max(mq2[:, 0:1], pn[0][0:H, :],
                                               axis=AX.X))
                        s.dve.inc(e.reduce_max(mk2[:, 0:1], pn[1][0:H, :],
                                               axis=AX.X))
                        e.wait_ge(s.dve.h, d0 + 2)
                        s.dve.inc(e.tensor_tensor(mq2[:, 0:1], mq2[:, 0:1],
                                                  mk2[:, 0:1], ALU.mult))
            with ExitStack() as pctx:
                pti = pctx.enter_context(nc.psum_tensor("ptib" + str(causal), [128, H],
                                                        F32))
                a1, d1, p1 = s.act.v, s.dve.v, s.pe.v
                with nc.Block() as blk:
                    @blk.scalar
                    def _(e):
                        s.dve.wait(e, d1)
                        s.act.inc(e.activation(mq2[:, 0:1], mq2[:, 0:1],
                                               AF.Ln, bias=t.epsc[0:H, 1:2]))
                        e.wait_ge(s.act.h, a1 + 1)
                        s.act.inc(e.activation(mq2[:, 0:1], mq2[:, 0:1],
                                               AF.Exp, scale=0.5))

                    @blk.vector
                    def _(e):
                        e.wait_ge(s.act.h, a1 + 2)
                        dd = s.dve.v
                        if causal:
                            s.dve.inc(e.tensor_scalar(
                                mq2[:, 0:1], mq2[:, 0:1], t.bmax16[:, 0:1],
                                1.0, ALU.add, ALU.add))
                        else:
                            s.dve.inc(e.tensor_scalar_add(
                                mq2[:, 0:1], mq2[:, 0:1], 1.0))
                        e.wait_ge(s.dve.h, dd + 1)
                        s.dve.inc(e.tensor_scalar_mul(mq2[:, 0:1],
                                                      mq2[:, 0:1], -1.0))

                    @blk.tensor
                    def _(e):
                        s.dve.wait(e)
                        s.pe.inc(e.transpose(pti[0:1, 0:H], mq2[0:H, 0:1],
                                             t.ident[0:H, 0:H]))

                    @blk.scalar
                    def _(e):
                        e.wait_ge(s.pe.h, p1 + 1)
                        s.act.inc(e.copy(brow[:, :], pti[0:1, 0:H]))

                    @blk.tensor
                    def _(e):
                        e.wait_ge(s.act.h, a1 + 3)
                        s.pe.inc(e.matmul(pti[:, 0:H], t.ones1f[:, :],
                                          brow[:, :], start=True, stop=True))

                    @blk.scalar
                    def _(e):
                        e.wait_ge(s.pe.h, p1 + 2)
                        s.act.inc(e.copy(t.negb[:, 0:H], pti[:, 0:H]))
        self.barrier()

    def phase_attn(self, causal):
        nc, s, t, scr = self.nc, self.s, self.t, self.scr
        with ExitStack() as ctx:
            probsT = ctx.enter_context(nc.sbuf_tensor(
                "probsT" + str(causal), [128, S], BF16))
            nrow = [ctx.enter_context(nc.sbuf_tensor(
                f"nrow{causal}{i}", [1, S], F32)) for i in range(2)]
            rsb = [ctx.enter_context(nc.sbuf_tensor(
                f"rsb{causal}{i}", [1, S], F32)) for i in range(2)]
            rbcf = [ctx.enter_context(nc.sbuf_tensor(
                f"rbcf{causal}{i}", [128, 1024], F32)) for i in range(2)]
            psS = [ctx.enter_context(nc.psum_tensor(
                f"psS{causal}{i}", [128, 1024], F32)) for i in range(2)]
            psC = ctx.enter_context(nc.psum_tensor(
                "psC" + str(causal), [VP, S], F32))
            m2 = ([ctx.enter_context(nc.sbuf_tensor(f"m2_{i}", [128, 4096],
                                                    BF16)) for i in range(2)]
                  if causal else None)

            sAv, sMs, sEvc = s.sb, s.sc, s.sd
            av0, ms0, ev0 = sAv.v, sMs.v, sEvc.v
            pe0, act0, sp0 = s.pe.v, s.act.v, s.sp.v

            units = []
            for kt in range(16):
                for qh in range(2):
                    if causal and qh * 1024 + 1023 < kt * 128:
                        continue
                    units.append((kt, qh))
            nU = len(units)
            APH = nU + 3

            def last_kt(qh):
                return min(15, 8 * qh + 7) if causal else 15

            def emit_av(e, h, u):
                kt, qh = units[u]
                if u == 0 and h >= 1:
                    sEvc.wait(e, ev0 + h)
                s.act.wait(e, act0 + h * APH + u + 1)
                for j in range(2):
                    ins = e.matmul(
                        psC[0:VP, qh * 1024 + j * 512:
                            qh * 1024 + (j + 1) * 512],
                        t.v_tok[:, kt * H * VP + h * VP:
                                kt * H * VP + (h + 1) * VP],
                        probsT[:, qh * 1024 + j * 512:
                               qh * 1024 + (j + 1) * 512],
                        start=(kt == 0), stop=(kt == last_kt(qh)))
                sAv.inc(ins)

            with nc.Block() as blk:
                if causal:
                    @blk.sync
                    def _(e):
                        for h in range(H):
                            if h >= 2:
                                sMs.wait(e, ms0 + (h - 1) * nU)
                            s.sp.inc(e.dma_start(
                                m2[h % 2][:, :],
                                dap(scr.slined, h * SLINE,
                                    [[1, 128], [1, 4096]])), 16)
                            if h >= 2:
                                s.act.wait(e, act0 + (h - 1) * APH)
                                s.sp.inc(e.dma_start(
                                    dap(scr.recipd, (h - 2) * S,
                                        [[1, 1], [1, S]]),
                                    nrow[(h - 2) % 2][:, :]), 16)
                        for h in (H - 2, H - 1):
                            s.act.wait(e, act0 + (h + 1) * APH)
                            s.sp.inc(e.dma_start(
                                dap(scr.recipd, h * S, [[1, 1], [1, S]]),
                                nrow[h % 2][:, :]), 16)
                else:
                    @blk.sync
                    def _(e):
                        for h in range(H):
                            if h >= 2:
                                s.act.wait(e, act0 + (h - 1) * APH)
                                s.sp.inc(e.dma_start(
                                    dap(scr.recipd, (h - 2) * S,
                                        [[1, 1], [1, S]]),
                                    nrow[(h - 2) % 2][:, :]), 16)
                        for h in (H - 2, H - 1):
                            s.act.wait(e, act0 + (h + 1) * APH)
                            s.sp.inc(e.dma_start(
                                dap(scr.recipd, h * S, [[1, 1], [1, S]]),
                                nrow[h % 2][:, :]), 16)

                @blk.tensor
                def _(e):
                    for h in range(H):
                        c, rb = h // 2, (h % 2) * 64
                        if causal:
                            s.sp.wait(e, sp0 + 16 * (2 * h if h >= 1
                                                     else 1))
                        if h >= 1:
                            s.act.wait(e, act0 + (h - 1) * APH + nU)
                        for u, (kt, qh) in enumerate(units):
                            if u >= 2:
                                s.act.wait(e, act0 + h * APH + u - 1)
                            ps = psS[u % 2]
                            for j in range(2):
                                ins = e.matmul(
                                    ps[:, j * 512:(j + 1) * 512],
                                    t.k_bf[rb:rb + 64,
                                           c * S + kt * 128:
                                           c * S + (kt + 1) * 128],
                                    t.q_bf[rb:rb + 64,
                                           c * S + qh * 1024 + j * 512:
                                           c * S + qh * 1024 +
                                           (j + 1) * 512],
                                    start=True, stop=not causal)
                                if causal:
                                    v0 = 2048 - 128 * kt
                                    ins = e.matmul(
                                        ps[:, j * 512:(j + 1) * 512],
                                        t.flip[:, :],
                                        m2[h % 2][:, v0 + qh * 1024 +
                                                  j * 512:
                                                  v0 + qh * 1024 +
                                                  (j + 1) * 512],
                                        start=False, stop=True)
                            sMs.inc(ins)
                            if u >= 1:
                                emit_av(e, h, u - 1)
                        emit_av(e, h, nU - 1)

                @blk.scalar
                def _(e):
                    for h in range(H):
                        c, rb = h // 2, (h % 2) * 64
                        for u, (kt, qh) in enumerate(units):
                            sMs.wait(e, ms0 + h * nU + u + 1)
                            v = -1
                            for vv in range(u - 1, -1, -1):
                                if units[vv][1] == qh:
                                    v = vv
                                    break
                            if v >= 0:
                                sAv.wait(e, av0 + h * nU + v + 1)
                            elif h >= 1:
                                sAv.wait(e, av0 + h * nU)
                            s.act.inc(e.activation(
                                probsT[:, qh * 1024:(qh + 1) * 1024],
                                psS[u % 2][:, :], AF.Exp,
                                bias=t.negb[:, h:h + 1], scale=1.0))
                        sAv.wait(e, av0 + (h + 1) * nU)
                        s.act.inc(e.copy(
                            t.ctx_bf[rb:rb + 64, c * S:(c + 1) * S],
                            psC[0:64, :]))
                        if h >= 2:
                            s.sp.wait(e, sp0 + 16 * (2 * h) if causal
                                      else sp0 + 16 * (h - 1))
                        nr = nrow[h % 2]
                        sEvc.inc(e.copy(nr[:, :], psC[64:65, :]))
                        aa = s.act.v
                        s.act.inc(e.activation(nr[:, :], nr[:, :], AF.Ln))
                        e.wait_ge(s.act.h, aa + 1)
                        s.act.inc(e.activation(nr[:, :], nr[:, :],
                                               AF.Exp, scale=-1.0))

            a1, p1, d1, sp1 = s.act.v, s.pe.v, s.dve.v, s.sp.v
            with nc.Block() as blk:
                @blk.sync
                def _(e):
                    for h in range(H):
                        if h >= 2:
                            e.wait_ge(s.pe.h, p1 + 2 * (h - 1))
                        s.sa.inc(e.dma_start(rsb[h % 2][:, :],
                                             dap(scr.recipd, h * S,
                                                 [[1, 1], [1, S]])), 16)

                @blk.tensor
                def _(e):
                    sa1 = s.sa.v - 16 * H
                    for h in range(H):
                        rb = (h % 2) * 64
                        e.wait_ge(s.sa.h, sa1 + 16 * (h + 1))
                        if h >= 2 and h % 2 == 0:
                            e.wait_ge(s.act.h, a1 + h)
                        for half in range(2):
                            for j in range(2):
                                ins = e.matmul(
                                    psS[half][rb:rb + 64,
                                              j * 512:(j + 1) * 512],
                                    t.ones1f[:, 0:64],
                                    rsb[h % 2][:, half * 1024 + j * 512:
                                               half * 1024 + (j + 1) * 512],
                                    start=True, stop=True)
                            s.pe.inc(ins)

                @blk.scalar
                def _(e):
                    for h in range(H):
                        if h % 2 == 1:
                            p = h // 2
                            e.wait_ge(s.pe.h, p1 + 2 * (h + 1))
                            for half in range(2):
                                if 2 * p + half >= 2:
                                    e.wait_ge(s.dve.h, d1 + 2 * p + half - 1)
                                s.act.inc(e.copy(rbcf[half][:, :],
                                                 psS[half][:, :]))

                @blk.vector
                def _(e):
                    for h in range(H):
                        if h % 2 == 1:
                            c = h // 2
                            for half in range(2):
                                e.wait_ge(s.act.h, a1 + (h // 2) * 2 +
                                          half + 1)
                                s.dve.inc(e.tensor_tensor(
                                    t.ctx_bf[:, c * S + half * 1024:
                                             c * S + (half + 1) * 1024],
                                    t.ctx_bf[:, c * S + half * 1024:
                                             c * S + (half + 1) * 1024],
                                    rbcf[half][:, :], ALU.mult))
        self.barrier()

    def phase_out_ln(self, block):
        nc, s, t, scr, inp = self.nc, self.s, self.t, self.scr, self.inp
        wsrc = inp.wo if block == "A" else inp.wproj
        with ExitStack() as octx:
            ohtok = octx.enter_context(nc.sbuf_tensor(
                "ohtok" + block, [128, 16 * E], BF16))
            xtokall = (octx.enter_context(nc.sbuf_tensor(
                "xtokall", [128, 16 * 1024], BF16)) if block == "B" else None)
            with ExitStack() as ctx:
                L = SimpleNamespace()
                c_ = ctx.enter_context
                L.wsb = c_(nc.sbuf_tensor("wout" + block, [128, CH * D], BF16))
                L.xs32 = c_(nc.sbuf_tensor("xs32" + block, [128, CH * 512], F32))
                L.xp = c_(nc.sbuf_tensor("xp" + block, [128, CH * 512], F32))
                L.tf = c_(nc.sbuf_tensor("tf" + block, [128, 512], F32))
                L.tb = [c_(nc.sbuf_tensor(f"tbo{block}{i}", [128, 512], BF16))
                        for i in range(2)]
                L.tb2 = [c_(nc.sbuf_tensor(f"tb2o{block}{i}", [128, 512], BF16))
                         for i in range(2)]
                L.rows = c_(nc.sbuf_tensor("rows" + block, [1, 512 * 5], F32))
                L.xlb = c_(nc.sbuf_tensor("xlb" + block, [128, CH * 512], BF16))
                L.rbuf = (c_(nc.sbuf_tensor("rbuf", [128, CH * 512], F32))
                          if block == "B" else None)
                L.gsc = c_(nc.sbuf_tensor("gsc" + block, [128, 2 + E], F32))
                L.glog = c_(nc.sbuf_tensor("glog" + block, [128, 16 * E],
                                           F32))
                L.psW = [c_(nc.psum_tensor(f"psW{block}{i}", [128, 512], F32))
                         for i in range(2)]
                L.pS1 = c_(nc.psum_tensor("pS1" + block, [1, 512], F32))
                L.pS2 = c_(nc.psum_tensor("pS2" + block, [1, 512], F32))
                L.pB1 = c_(nc.psum_tensor("pB1" + block, [128, 512], F32))
                L.pB2 = c_(nc.psum_tensor("pB2" + block, [128, 512], F32))
                L.pG = c_(nc.psum_tensor("pG" + block, [128, E], F32))
                if block == "B":
                    L.pTb = c_(nc.psum_tensor("pTb", [128, 1024], BF16))
                    L.xtokall = xtokall
                L.ohtok = ohtok

                L.wl0 = s.gp.v
                with nc.Block() as blk:
                    @blk.gpsimd
                    def _(e):
                        s.gp.inc(e.dma_start(
                            L.wsb[:, :],
                            dap(wsrc, 0, [[D, 128], [128 * D, CH], [1, D]])),
                            16)

                for strip in range(4):
                    self._outln_strip(block, strip, L)
                    self.barrier()
                if block == "B" and "glog" in self.dbg:
                    self.dump("glog", L.glog[:, :], [128, 16 * E])
            if block == "B":
                self._routing(ohtok, xtokall)
        self.barrier()

    def _row(self, L, i):
        return L.rows[:, i * 512:(i + 1) * 512]

    def _stats_pass(self, src_getter, L):
        nc, s, t = self.nc, self.s, self.t
        sCp, sQq = s.sj, s.sk
        cp0, qq0 = sCp.v, sQq.v
        pe1, act1, dve1 = s.pe.v, s.act.v, s.dve.v
        sc1 = s.sc.v
        with nc.Block() as blk:
            @blk.scalar
            def _(e):
                e.wait_ge(s.sc.h, sc1)
                e.wait_ge(s.dve.h, dve1)
                for cout in range(CH):
                    if cout >= 2:
                        e.wait_ge(s.pe.h, pe1 + cout - 1)
                    sCp.inc(e.copy(L.tb[cout % 2][:, :], src_getter(cout)))
                    sQq.inc(e.activation(L.tb2[cout % 2][:, :],
                                         src_getter(cout), AF.Square))

            @blk.tensor
            def _(e):
                for cout in range(CH):
                    sCp.wait(e, cp0 + cout + 1)
                    e.matmul(L.pS1[0:1, :], t.ones128[:, 0:1],
                             L.tb[cout % 2][:, :],
                             start=(cout == 0), stop=(cout == CH - 1))
                    sQq.wait(e, qq0 + cout + 1)
                    ins = e.matmul(L.pS2[0:1, :], t.ones128[:, 0:1],
                                   L.tb2[cout % 2][:, :],
                                   start=(cout == 0), stop=(cout == CH - 1))
                    s.pe.inc(ins)

            @blk.vector
            def _(e):
                e.wait_ge(s.pe.h, pe1 + CH)
                dd = s.dve.v
                s.dve.inc(e.tensor_scalar_mul(self._row(L, 0), L.pS1[0:1, :],
                                              1.0 / D))
                s.dve.inc(e.tensor_scalar_mul(self._row(L, 4), L.pS2[0:1, :],
                                              1.0 / D))
                e.wait_ge(s.dve.h, dd + 2)
                s.dve.inc(e.tensor_tensor(self._row(L, 3), self._row(L, 0),
                                          self._row(L, 0), ALU.mult))
                e.wait_ge(s.dve.h, dd + 3)
                s.dve.inc(e.tensor_tensor(self._row(L, 4), self._row(L, 4),
                                          self._row(L, 3), ALU.subtract))

            @blk.scalar
            def _(e):
                e.wait_ge(s.dve.h, dve1 + 4)
                aa = s.act.v
                s.act.inc(e.activation(self._row(L, 1), self._row(L, 4),
                                       AF.Ln, bias=t.epsc[0:1, 0:1]))
                e.wait_ge(s.act.h, aa + 1)
                s.act.inc(e.activation(self._row(L, 1), self._row(L, 1),
                                       AF.Exp, scale=-0.5))

            @blk.vector
            def _(e):
                e.wait_ge(s.act.h, act1 + 2)
                dd = s.dve.v
                s.dve.inc(e.tensor_tensor(self._row(L, 2), self._row(L, 0),
                                          self._row(L, 1), ALU.mult))
                e.wait_ge(s.dve.h, dd + 1)
                s.dve.inc(e.tensor_scalar_mul(self._row(L, 2),
                                              self._row(L, 2), -1.0))

            @blk.tensor
            def _(e):
                s.dve.wait(e)
                e.matmul(L.pB1[:, :], t.ones1f[:, :], self._row(L, 1),
                         start=True, stop=True)
                ins = e.matmul(L.pB2[:, :], t.ones1f[:, :], self._row(L, 2),
                               start=True, stop=True)
                s.pe.inc(ins)

    def _outln_strip(self, block, strip, L):
        nc, s, t, scr, inp = self.nc, self.s, self.t, self.scr, self.inp
        so = strip * 512
        x_src = scr.xfm32 if block == "A" else scr.x1d
        sL, sMm, sEv = s.sa, s.sb, s.sc
        l0, mm0, ev0 = sL.v, sMm.v, sEv.v
        lnoff = 0 if block == "A" else 2 * CH

        with nc.Block() as blk:
            @blk.sync
            def _(e):
                sL.inc(e.dma_start(
                    L.xs32[:, :],
                    dap(x_src, so, [[S, 128], [128 * S, CH], [1, 512]])), 16)

            @blk.tensor
            def _(e):
                e.wait_ge(s.gp.h, L.wl0 + 16)
                for cout in range(CH):
                    if cout >= 2:
                        sEv.wait(e, ev0 + cout - 1)
                    for cin in range(CH):
                        ins = e.matmul(
                            L.psW[cout % 2][:, :],
                            L.wsb[:, cin * D + cout * 128:
                                  cin * D + (cout + 1) * 128],
                            t.ctx_bf[:, cin * S + so:cin * S + so + 512],
                            start=(cin == 0), stop=(cin == CH - 1))
                    sMm.inc(ins)

            @blk.vector
            def _(e):
                sL.wait(e, l0 + 16)
                for cout in range(CH):
                    sMm.wait(e, mm0 + cout + 1)
                    sEv.inc(e.tensor_tensor(
                        L.xp[:, cout * 512:(cout + 1) * 512],
                        L.psW[cout % 2][:, :],
                        L.xs32[:, cout * 512:(cout + 1) * 512], ALU.add))

        self._stats_pass(lambda cout: L.xp[:, cout * 512:(cout + 1) * 512], L)

        if block == "A":
            d2 = s.dve.v
            with nc.Block() as blk:
                @blk.vector
                def _(e):
                    s.pe.wait(e)
                    for cout in range(CH):
                        dd = s.dve.v
                        s.dve.inc(e.tensor_tensor(
                            L.tf[:, :], L.xp[:, cout * 512:(cout + 1) * 512],
                            L.pB1[:, :], ALU.mult))
                        e.wait_ge(s.dve.h, dd + 1)
                        s.dve.inc(e.tensor_tensor(
                            L.xp[:, cout * 512:(cout + 1) * 512],
                            L.tf[:, :], L.pB2[:, :], ALU.add))

                @blk.sync
                def _(e):
                    s.dve.wait(e)
                    s.sp.inc(e.dma_start(
                        dap(scr.x1d, so, [[S, 128], [128 * S, CH], [1, 512]]),
                        L.xp[:, :]), 16)
            self._stats_pass(
                lambda cout: L.xp[:, cout * 512:(cout + 1) * 512], L)
            with nc.Block() as blk:
                @blk.vector
                def _(e):
                    s.pe.wait(e)
                    for cout in range(CH):
                        dd = s.dve.v
                        s.dve.inc(e.tensor_tensor(
                            L.tf[:, :], L.xp[:, cout * 512:(cout + 1) * 512],
                            L.pB1[:, :], ALU.mult))
                        e.wait_ge(s.dve.h, dd + 1)
                        s.dve.inc(e.tensor_tensor(
                            L.tf[:, :], L.tf[:, :], L.pB2[:, :], ALU.add))
                        e.wait_ge(s.dve.h, dd + 2)
                        s.dve.inc(e.tensor_scalar(
                            t.xln1_bf[:, cout * S + so:cout * S + so + 512],
                            L.tf[:, :], t.lnw[:, cout:cout + 1],
                            t.lnw[:, CH + cout:CH + cout + 1],
                            ALU.mult, ALU.add))
        else:
            with nc.Block() as blk:
                @blk.vector
                def _(e):
                    s.pe.wait(e)
                    for cout in range(CH):
                        dd = s.dve.v
                        s.dve.inc(e.tensor_tensor(
                            L.rbuf[:, cout * 512:(cout + 1) * 512],
                            L.xp[:, cout * 512:(cout + 1) * 512],
                            L.xs32[:, cout * 512:(cout + 1) * 512], ALU.add))
                        s.dve.inc(e.tensor_tensor(
                            L.tf[:, :], L.xp[:, cout * 512:(cout + 1) * 512],
                            L.pB1[:, :], ALU.mult))
                        e.wait_ge(s.dve.h, dd + 2)
                        s.dve.inc(e.tensor_tensor(
                            L.tf[:, :], L.tf[:, :], L.pB2[:, :], ALU.add))
                        e.wait_ge(s.dve.h, dd + 3)
                        s.dve.inc(e.tensor_scalar(
                            L.xlb[:, cout * 512:(cout + 1) * 512],
                            L.tf[:, :], t.lnw[:, lnoff + cout:lnoff + cout + 1],
                            t.lnw[:, lnoff + CH + cout:lnoff + CH + cout + 1],
                            ALU.mult, ALU.add))

                @blk.sync
                def _(e):
                    s.dve.wait(e)
                    s.sp.inc(e.dma_start(
                        dap(scr.rd, so, [[S, 128], [128 * S, CH], [1, 512]]),
                        L.rbuf[:, :]), 16)

            if block != "B":
                return
            L.tsp0 = s.act.v
            with nc.Block() as blk:
                @blk.tensor
                def _(e):
                    s.dve.wait(e)
                    for tt in range(4):
                        if tt >= 1:
                            e.wait_ge(s.act.h, L.tsp0 + tt)
                        for c in range(CH):
                            ins = e.transpose(
                                L.pTb[:, c * 128:(c + 1) * 128],
                                L.xlb[:, c * 512 + tt * 128:
                                      c * 512 + (tt + 1) * 128],
                                t.identb[:, :])
                        s.pe.inc(ins)

                @blk.scalar
                def _(e):
                    pe9 = s.pe.v - 4
                    for tt in range(4):
                        e.wait_ge(s.pe.h, pe9 + tt + 1)
                        g = strip * 4 + tt
                        s.act.inc(e.copy(
                            L.xtokall[:, g * 1024:(g + 1) * 1024],
                            L.pTb[:, :]))

            with nc.Block() as blk:
                @blk.tensor
                def _(e):
                    s.dve.wait(e)
                    g0 = s.dve.v
                    for tt in range(4):
                        if tt >= 1:
                            e.wait_ge(s.dve.h, g0 + 7 * tt)
                        for cin in range(CH):
                            ins = e.matmul(
                                L.pG[:, :],
                                L.xlb[:, cin * 512 + tt * 128:
                                      cin * 512 + (tt + 1) * 128],
                                t.gws[:, cin * E:(cin + 1) * E],
                                start=(cin == 0), stop=(cin == CH - 1))
                        s.pe.inc(ins)

                @blk.vector
                def _(e):
                    pe2 = s.pe.v - 4
                    gl = L.gsc[:, 2:2 + E]
                    e.wait_ge(s.act.h, L.tsp0 + 4)
                    for tt in range(4):
                        s.pe.wait(e, pe2 + tt + 1)
                        dd = s.dve.v
                        s.dve.inc(e.tensor_tensor(gl, L.pG[:, :],
                                                  t.gbb[:, :], ALU.add))
                        e.wait_ge(s.dve.h, dd + 1)
                        s.dve.inc(e.tensor_copy(
                            L.glog[:, (strip * 4 + tt) * E:
                                   (strip * 4 + tt + 1) * E], gl))
                        s.dve.inc(e.reduce_max(L.gsc[:, 0:1], gl, axis=AX.X))
                        e.wait_ge(s.dve.h, dd + 3)
                        s.dve.inc(e.tensor_scalar(gl, gl, L.gsc[:, 0:1],
                                                  None, ALU.is_equal))
                        e.wait_ge(s.dve.h, dd + 4)
                        s.dve.inc(e.tensor_tensor(gl, gl, t.revw[:, :],
                                                  ALU.mult))
                        e.wait_ge(s.dve.h, dd + 5)
                        s.dve.inc(e.reduce_max(L.gsc[:, 1:2], gl, axis=AX.X))
                        e.wait_ge(s.dve.h, dd + 6)
                        s.dve.inc(e.tensor_scalar(
                            L.ohtok[:, (strip * 4 + tt) * E:
                                    (strip * 4 + tt + 1) * E],
                            gl, L.gsc[:, 1:2], None, ALU.is_equal))

    def _routing(self, ohtok, xtokall):
        from concourse.bass_core_rust import IndirectOffsetOnAxis
        nc, s, t, scr = self.nc, self.s, self.t, self.scr
        with ExitStack() as ctx:
            c_ = ctx.enter_context
            rank = c_(nc.sbuf_tensor("ranksb", [128, 16 * E], F32))
            tmpE = c_(nc.sbuf_tensor("tmpE", [128, E], F32))
            cols = c_(nc.sbuf_tensor("rcols", [128, 8], F32))
            slotf = c_(nc.sbuf_tensor("slotf", [128, 16], F32))
            erows = c_(nc.sbuf_tensor("erows", [128, 16], F32))
            rrows = c_(nc.sbuf_tensor("rrows", [128, 16], F32))
            osf = c_(nc.sbuf_tensor("osf", [128, 16], F32))
            osb = c_(nc.sbuf_tensor("osb", [128, 16], BF16))
            crow = c_(nc.sbuf_tensor("crow", [1, 16], F32))
            ccol = c_(nc.sbuf_tensor("ccol", [16, 1], F32))
            ccol2 = c_(nc.sbuf_tensor("ccol2", [16, 1], F32))
            crow2 = c_(nc.sbuf_tensor("crow2", [1, 16], F32))
            pR = [c_(nc.psum_tensor(f"pR{i}", [128, E], F32))
                  for i in range(2)]
            pP = c_(nc.psum_tensor("pP", [128, 16], F32))
            pCs = c_(nc.psum_tensor("pCs", [1, 16], F32))
            pCc = c_(nc.psum_tensor("pCc", [16, 1], F32))
            pCm = c_(nc.psum_tensor("pCm", [16, 1], F32))
            pCr = c_(nc.psum_tensor("pCr", [1, 16], F32))
            pe0, act0, dve0 = s.pe.v, s.act.v, s.dve.v
            with nc.Block() as blk:
                @blk.tensor
                def _(e):
                    for k in range(16):
                        if k >= 2:
                            e.wait_ge(s.act.h, act0 + k - 1)
                        for j in range(k + 1):
                            ins = e.matmul(
                                pR[k % 2][:, :],
                                t.allon[:, :] if j < k else t.triuS[:, :],
                                ohtok[:, j * E:(j + 1) * E],
                                start=(j == 0), stop=(j == k))
                        s.pe.inc(ins)
                    e.wait_ge(s.dve.h, dve0 + 96)
                    s.pe.inc(e.matmul(pCs[0:1, :], t.ones128[:, :],
                                      osb[:, :], start=True, stop=True))
                    e.wait_ge(s.act.h, act0 + 17)
                    s.pe.inc(e.transpose(pCc[:, :], crow[0:1, :],
                                         t.ident[0:1, 0:1]))
                    e.wait_ge(s.act.h, act0 + 18)
                    s.pe.inc(e.matmul(pCm[:, :], t.triuSf[:, :],
                                      ccol[:, :], start=True, stop=True))
                    e.wait_ge(s.act.h, act0 + 19)
                    s.pe.inc(e.transpose(pCr[0:1, :], ccol2[:, :],
                                         t.ident[0:16, 0:16]))
                    e.wait_ge(s.act.h, act0 + 20)
                    e.matmul(pP[:, :], t.triuS[:, :], osb[:, :],
                             start=True, stop=False)
                    s.pe.inc(e.matmul(pP[:, :], t.ones1f[:, :],
                                      crow2[0:1, :], start=False, stop=True))

                @blk.scalar
                def _(e):
                    for k in range(16):
                        e.wait_ge(s.pe.h, pe0 + k + 1)
                        s.act.inc(e.copy(rank[:, k * E:(k + 1) * E],
                                         pR[k % 2][:, :]))
                    e.wait_ge(s.pe.h, pe0 + 17)
                    s.act.inc(e.copy(crow[0:1, :], pCs[0:1, :]))
                    e.wait_ge(s.pe.h, pe0 + 18)
                    s.act.inc(e.copy(ccol[:, :], pCc[:, :]))
                    e.wait_ge(s.pe.h, pe0 + 19)
                    s.act.inc(e.copy(ccol2[:, :], pCm[:, :]))
                    e.wait_ge(s.pe.h, pe0 + 20)
                    s.act.inc(e.copy(crow2[0:1, :], pCr[0:1, :]))

                @blk.vector
                def _(e):
                    for k in range(16):
                        e.wait_ge(s.act.h, act0 + k + 1)
                        dd = s.dve.v
                        s.dve.inc(e.tensor_tensor(
                            tmpE[:, :], ohtok[:, k * E:(k + 1) * E],
                            t.iotaE[:, :], ALU.mult))
                        e.wait_ge(s.dve.h, dd + 1)
                        s.dve.inc(e.reduce_sum(erows[:, k:k + 1], tmpE[:, :],
                                               axis=AX.X))
                        s.dve.inc(e.tensor_tensor(
                            tmpE[:, :], ohtok[:, k * E:(k + 1) * E],
                            rank[:, k * E:(k + 1) * E], ALU.mult))
                        e.wait_ge(s.dve.h, dd + 3)
                        s.dve.inc(e.reduce_sum(rrows[:, k:k + 1], tmpE[:, :],
                                               axis=AX.X))
                        e.wait_ge(s.dve.h, dd + 4)
                        s.dve.inc(e.tensor_single_scalar(
                            osf[:, k:k + 1], rrows[:, k:k + 1], 511.5,
                            ALU.is_gt))
                        e.wait_ge(s.dve.h, dd + 5)
                        s.dve.inc(e.tensor_copy(osb[:, k:k + 1],
                                                osf[:, k:k + 1]))
                    e.wait_ge(s.pe.h, pe0 + 21)
                    e.wait_ge(s.act.h, act0 + 20)
                    for k in range(16):
                        dd = s.dve.v
                        s.dve.inc(e.tensor_scalar(
                            cols[:, 0:1], erows[:, k:k + 1], 512.0,
                            rrows[:, k:k + 1], ALU.mult, ALU.add))
                        s.dve.inc(e.tensor_single_scalar(
                            cols[:, 1:2], pP[:, k:k + 1], 4096.0, ALU.add))
                        s.dve.inc(e.tensor_single_scalar(
                            cols[:, 2:3], pP[:, k:k + 1], 255.5, ALU.is_gt))
                        e.wait_ge(s.dve.h, dd + 3)
                        s.dve.inc(e.scalar_tensor_tensor(
                            cols[:, 3:4], cols[:, 2:3], 100000.0,
                            cols[:, 1:2], ALU.mult, ALU.add))
                        e.wait_ge(s.dve.h, dd + 4)
                        s.dve.inc(e.tensor_tensor(
                            cols[:, 4:5], cols[:, 3:4], cols[:, 0:1],
                            ALU.subtract))
                        e.wait_ge(s.dve.h, dd + 5)
                        s.dve.inc(e.tensor_tensor(
                            cols[:, 5:6], osf[:, k:k + 1], cols[:, 4:5],
                            ALU.mult))
                        e.wait_ge(s.dve.h, dd + 6)
                        s.dve.inc(e.tensor_tensor(
                            slotf[:, k:k + 1], cols[:, 0:1], cols[:, 5:6],
                            ALU.add))
                        e.wait_ge(s.dve.h, dd + 7)
                        s.dve.inc(e.tensor_copy(t.sloti[:, k:k + 1],
                                                slotf[:, k:k + 1]))

                @blk.gpsimd
                def _(e):
                    for k in range(16):
                        e.wait_ge(s.dve.h, dve0 + 96 + 8 * (k + 1))
                        s.gp.inc(e.indirect_dma_start(
                            dap(scr.xcomp, 0, [[D, 4352], [1, D]]),
                            IndirectOffsetOnAxis(t.sloti[:, k:k + 1], 0),
                            xtokall[:, k * 1024:(k + 1) * 1024],
                            None), 16)
                        s.gp.inc(e.indirect_dma_start(
                            dap(scr.pexpd, 0, [[1, 4352], [1, 1]]),
                            IndirectOffsetOnAxis(t.sloti[:, k:k + 1], 0),
                            erows[:, k:k + 1],
                            None), 16)
        self.barrier()

    def phase_moe_final(self):
        nc, s, t, scr, inp = self.nc, self.s, self.t, self.scr, self.inp
        NT_E = 4
        SW = NT_E * 128
        PSL = 256
        SWE = SW + PSL
        SEGS = ((0, SW), (SW, PSL))
        with ExitStack() as ctx:
            c_ = ctx.enter_context
            xct = [c_(nc.sbuf_tensor(f"xct{i}", [128, D], BF16))
                   for i in range(2)]
            xcfm = c_(nc.sbuf_tensor("xcfm", [128, CH * SWE], BF16))
            xpfm = c_(nc.sbuf_tensor("xpfm", [128, CH * PSL], BF16))
            maskb = c_(nc.sbuf_tensor("maskb", [128, E * PSL], BF16))
            mrows = c_(nc.sbuf_tensor("mrows", [1, E * PSL], F32))
            pexps = c_(nc.sbuf_tensor("pexps", [1, PSL], F32))
            h_sb = c_(nc.sbuf_tensor("h_sbr", [128, 32 * SWE], BF16))
            w1q = [c_(nc.sbuf_tensor(f"w1qr{i}", [128, CH * 1024], BF16))
                   for i in range(2)]
            w2cb = [c_(nc.sbuf_tensor(f"w2cbr{i}", [128, 32 * 128], BF16))
                    for i in range(2)]
            yfm = c_(nc.sbuf_tensor("yfm", [128, CH * SW], F32))
            ypacc = c_(nc.sbuf_tensor("ypacc", [128, CH * PSL], F32))
            yptmp = [c_(nc.sbuf_tensor(f"yptmp{i}", [128, PSL], F32))
                     for i in range(2)]
            ytk = [c_(nc.sbuf_tensor(f"ytk{i}", [128, D], F32))
                   for i in range(2)]
            pH = [c_(nc.psum_tensor(f"pHr{i}", [128, 512], F32))
                  for i in range(2)]
            pY = [c_(nc.psum_tensor(f"pYr{i}", [128, 512], F32))
                  for i in range(2)]
            pTx = c_(nc.psum_tensor("pTxr", [128, 1024], BF16))
            pTy = c_(nc.psum_tensor("pTyr", [128, 1024], F32))
            pM = c_(nc.psum_tensor("pMr", [128, PSL], F32))

            sa0, pe0p, act0p, dve0p = s.sa.v, s.pe.v, s.act.v, s.dve.v
            with nc.Block() as blk:
                @blk.sync
                def _(e):
                    s.sa.inc(e.dma_start(
                        pexps[0:1, :],
                        dap(scr.pexpd, 4096, [[1, 1], [1, PSL]])), 16)
                    for tl in range(2):
                        s.sa.inc(e.dma_start(
                            xct[tl][:, :],
                            dap(scr.xcomp, (4096 + tl * 128) * D,
                                [[D, 128], [1, D]])), 16)

                @blk.tensor
                def _(e):
                    for tl in range(2):
                        e.wait_ge(s.sa.h, sa0 + 16 * (tl + 2))
                        if tl == 1:
                            e.wait_ge(s.act.h, act0p + 1)
                        for c in range(CH):
                            ins = e.transpose(
                                pTx[:, c * 128:(c + 1) * 128],
                                xct[tl][:, c * 128:(c + 1) * 128],
                                t.identb[:, :])
                        s.pe.inc(ins)
                    for ee in range(E):
                        e.wait_ge(s.dve.h, dve0p + ee + 2)
                        if ee >= 1:
                            e.wait_ge(s.act.h, act0p + 2 + ee)
                        s.pe.inc(e.matmul(
                            pM[:, :], t.ones1f[:, :],
                            mrows[0:1, ee * PSL:(ee + 1) * PSL],
                            start=True, stop=True))

                @blk.scalar
                def _(e):
                    for tl in range(2):
                        e.wait_ge(s.pe.h, pe0p + tl + 1)
                        s.act.inc(e.copy(
                            dap(xpfm, tl * 128,
                                [[CH * PSL, 128], [PSL, CH], [1, 128]]),
                            pTx[:, :]))
                    for ee in range(E):
                        e.wait_ge(s.pe.h, pe0p + 2 + ee + 1)
                        s.act.inc(e.copy(
                            maskb[:, ee * PSL:(ee + 1) * PSL], pM[:, :]))

                @blk.vector
                def _(e):
                    s.dve.inc(e.memset(ypacc[:, :], 0.0))
                    e.wait_ge(s.sa.h, sa0 + 16)
                    for ee in range(E):
                        s.dve.inc(e.tensor_single_scalar(
                            mrows[0:1, ee * PSL:(ee + 1) * PSL],
                            pexps[0:1, :], float(ee), ALU.is_equal))
            self.barrier()

            sXl, sTx, sHM, sH, sY, sYE, sTy, sYS = (s.sa, s.sb, s.sc, s.sd,
                                                    s.se, s.sf, s.sg, s.sh)
            xl0, tx0, hm0, h0, y0, ye0, ty0, ys0 = (sXl.v, sTx.v, sHM.v,
                                                    sH.v, sY.v, sYE.v,
                                                    sTy.v, sYS.v)
            gp0 = s.gp.v
            pe0, act0, dve1 = s.pe.v, s.act.v, s.dve.v
            def a_xc(g):
                return act0 + g + 1

            def pe_yt(ee, tl):
                return pe0 + NT_E * ee + tl + 1

            def w1_done(ee, q1):
                return gp0 + 16 * (ee * 12 + q1 + 1)

            def w2_done(ee, dc):
                return gp0 + 16 * (ee * 12 + 4 + dc + 1)

            def dve_add(gyp):
                ee2, dc2 = divmod(gyp, CH)
                return dve1 + 16 * ee2 + 8 + dc2 + 1

            with nc.Block() as blk:
                @blk.sync
                def _(e):
                    for ee in range(E):
                        for tl in range(NT_E):
                            g = ee * NT_E + tl
                            if g >= 2:
                                sTx.wait(e, tx0 + g - 1)
                            sXl.inc(e.dma_start(
                                xct[g % 2][:, :],
                                dap(scr.xcomp, (ee * SW + tl * 128) * D,
                                    [[D, 128], [1, D]])), 16)
                        for tl in range(NT_E):
                            g = ee * NT_E + tl
                            sTy.wait(e, ty0 + g + 1)
                            sYS.inc(e.dma_start(
                                dap(scr.ycomp, (ee * SW + tl * 128) * D,
                                    [[D, 128], [1, D]]),
                                ytk[g % 2][:, :]), 16)
                    for tl in range(2):
                        gg = 4 * E + tl
                        sTy.wait(e, ty0 + gg + 1)
                        sYS.inc(e.dma_start(
                            dap(scr.ycomp, (4096 + tl * 128) * D,
                                [[D, 128], [1, D]]),
                            ytk[gg % 2][:, :]), 16)

                @blk.gpsimd
                def _(e):
                    for ee in range(E):
                        for q1 in range(4):
                            gq = ee * 4 + q1
                            if gq >= 2:
                                sHM.wait(e, hm0 + (gq - 1) * 16)
                            s.gp.inc(e.dma_start(
                                w1q[gq % 2][:, :],
                                dap(inp.w1, ee * D * HID + q1 * 1024,
                                    [[HID, 128], [128 * HID, CH],
                                     [1, 1024]])), 16)
                        for dc in range(CH):
                            gd = ee * 8 + dc
                            if gd >= 2:
                                sY.wait(e, y0 + (gd - 1) * 2)
                            s.gp.inc(e.dma_start(
                                w2cb[gd % 2][:, :],
                                dap(inp.w2, ee * HID * D + dc * 128,
                                    [[D, 128], [128 * D, 32], [1, 128]])),
                                16)

                @blk.tensor
                def _(e):
                    for ee in range(E):
                        for tl in range(NT_E):
                            g = ee * NT_E + tl
                            sXl.wait(e, xl0 + 16 * (g + 1))
                            if g >= 1:
                                e.wait_ge(s.act.h, a_xc(g - 1))
                            for c in range(CH):
                                ins = e.transpose(
                                    pTx[:, c * 128:(c + 1) * 128],
                                    xct[g % 2][:, c * 128:(c + 1) * 128],
                                    t.identb[:, :])
                            sTx.inc(ins)
                        e.wait_ge(s.act.h, act0 + (ee + 1) * NT_E)
                        e.wait_ge(s.dve.h, dve1 + 16 * ee + 8)
                        for q1 in range(4):
                            e.wait_ge(s.gp.h, w1_done(ee, q1))
                            for hl in range(CH):
                                for sh_, (soff, slen) in enumerate(SEGS):
                                    ghl = ee * 64 + (q1 * 8 + hl) * 2 + sh_
                                    if ghl >= 2:
                                        sH.wait(e, h0 + ghl - 1)
                                    for cin in range(CH):
                                        ins = e.matmul(
                                            pH[ghl % 2][:, 0:slen],
                                            w1q[(ee * 4 + q1) % 2][
                                                :, cin * 1024 + hl * 128:
                                                cin * 1024 + (hl + 1) * 128],
                                            xcfm[:, cin * SWE + soff:
                                                 cin * SWE + soff + slen],
                                            start=(cin == 0),
                                            stop=(cin == CH - 1))
                                    sHM.inc(ins)
                        for dc in range(CH):
                            e.wait_ge(s.gp.h, w2_done(ee, dc))
                            sH.wait(e, h0 + 64 * (ee + 1))
                            for sh_, (soff, slen) in enumerate(SEGS):
                                gy = (ee * 8 + dc) * 2 + sh_
                                if gy >= 2:
                                    sYE.wait(e, ye0 + gy - 1)
                                for hh in range(32):
                                    ins = e.matmul(
                                        pY[gy % 2][:, 0:slen],
                                        w2cb[(ee * 8 + dc) % 2][
                                            :, hh * 128:(hh + 1) * 128],
                                        h_sb[:, hh * SWE + soff:
                                             hh * SWE + soff + slen],
                                        start=(hh == 0), stop=(hh == 31))
                                sY.inc(ins)
                        for tl in range(NT_E):
                            g = ee * NT_E + tl
                            sYE.wait(e, ye0 + 16 * (ee + 1))
                            if g >= 1:
                                sTy.wait(e, ty0 + g)
                            for c in range(CH):
                                ins = e.transpose(
                                    pTy[:, c * 128:(c + 1) * 128],
                                    yfm[:, c * SW + tl * 128:
                                        c * SW + (tl + 1) * 128],
                                    t.ident[:, :])
                            s.pe.inc(ins)
                    e.wait_ge(s.dve.h, dve1 + 16 * E)
                    for tl in range(2):
                        gg = 4 * E + tl
                        sTy.wait(e, ty0 + gg)
                        for c in range(CH):
                            ins = e.transpose(
                                pTy[:, c * 128:(c + 1) * 128],
                                ypacc[:, c * PSL + tl * 128:
                                      c * PSL + (tl + 1) * 128],
                                t.ident[:, :])
                        s.pe.inc(ins)

                @blk.scalar
                def _(e):
                    for ee in range(E):
                        for tl in range(NT_E):
                            g = ee * NT_E + tl
                            sTx.wait(e, tx0 + g + 1)
                            s.act.inc(e.copy(
                                dap(xcfm, tl * 128,
                                    [[CH * SWE, 128], [SWE, CH], [1, 128]]),
                                pTx[:, :]))
                        for i in range(64):
                            ghl = ee * 64 + i
                            sHM.wait(e, hm0 + ghl + 1)
                            hl_g, sh_ = divmod(i, 2)
                            soff, slen = SEGS[sh_]
                            sH.inc(e.activation(
                                h_sb[:, hl_g * SWE + soff:
                                     hl_g * SWE + soff + slen],
                                pH[ghl % 2][:, 0:slen], AF.Gelu_apprx_tanh))
                        for i in range(16):
                            gy = ee * 16 + i
                            dc, sh_ = divmod(i, 2)
                            sY.wait(e, y0 + gy + 1)
                            if sh_ == 0:
                                sYE.inc(e.copy(
                                    yfm[:, dc * SW:(dc + 1) * SW],
                                    pY[gy % 2][:, 0:SW]))
                            else:
                                gyp = ee * 8 + dc
                                if gyp >= 2:
                                    e.wait_ge(s.dve.h, dve_add(gyp - 2))
                                sYE.inc(e.copy(yptmp[gyp % 2][:, :],
                                               pY[gy % 2][:, 0:PSL]))
                        for tl in range(NT_E):
                            g = ee * NT_E + tl
                            e.wait_ge(s.pe.h, pe_yt(ee, tl))
                            if g >= 2:
                                sYS.wait(e, ys0 + 16 * (g - 1))
                            sTy.inc(e.copy(ytk[g % 2][:, :], pTy[:, :]))
                    for tl in range(2):
                        gg = 4 * E + tl
                        e.wait_ge(s.pe.h, pe0 + 4 * E + tl + 1)
                        sYS.wait(e, ys0 + 16 * (gg - 1))
                        sTy.inc(e.copy(ytk[gg % 2][:, :], pTy[:, :]))

                @blk.vector
                def _(e):
                    for ee in range(E):
                        if ee >= 1:
                            sHM.wait(e, hm0 + 64 * ee)
                        for cin in range(CH):
                            s.dve.inc(e.tensor_tensor(
                                xcfm[:, cin * SWE + SW:cin * SWE + SWE],
                                xpfm[:, cin * PSL:(cin + 1) * PSL],
                                maskb[:, ee * PSL:(ee + 1) * PSL],
                                ALU.mult))
                        for dc in range(CH):
                            gyp = ee * 8 + dc
                            sYE.wait(e, ye0 + ee * 16 + 2 * dc + 2)
                            s.dve.inc(e.tensor_tensor(
                                ypacc[:, dc * PSL:(dc + 1) * PSL],
                                ypacc[:, dc * PSL:(dc + 1) * PSL],
                                yptmp[gyp % 2][:, :], ALU.add))
        self.barrier()
        self._phase_final()

    def _phase_final(self):
        from concourse.bass_core_rust import IndirectOffsetOnAxis
        nc, s, t, scr = self.nc, self.s, self.t, self.scr
        with ExitStack() as ctx:
            c_ = ctx.enter_context
            rfm = [c_(nc.sbuf_tensor(f"rfm{i}", [128, CH * 128], F32))
                   for i in range(2)]
            ytok = [c_(nc.sbuf_tensor(f"ytokf{i}", [128, D], F32))
                    for i in range(2)]
            xo = c_(nc.sbuf_tensor("xo", [128, D], F32))
            sqf = c_(nc.sbuf_tensor("sqff", [128, D], F32))
            otok = [c_(nc.sbuf_tensor(f"otokf{i}", [128, D], F32))
                    for i in range(2)]
            cols = c_(nc.sbuf_tensor("colsf", [128, 4], F32))
            pT = c_(nc.psum_tensor("pTf", [128, 1024], F32))
            sRl, sT, sO, sV2, sSum = s.sa, s.sc, s.sd, s.se, s.sf
            r0, t0, o0, v20, su0 = sRl.v, sT.v, sO.v, sV2.v, sSum.v
            pe0 = s.pe.v
            dve_pz = s.dve.v
            with nc.Block() as blk:
                @blk.vector
                def _(e):
                    s.dve.inc(e.memset(ytok[0][:, :], 0.0))
                    s.dve.inc(e.memset(ytok[1][:, :], 0.0))

                @blk.sync
                def _(e):
                    for tl in range(16):
                        if tl >= 2:
                            e.wait_ge(s.pe.h, pe0 + tl - 1)
                        sRl.inc(e.dma_start(
                            rfm[tl % 2][:, :],
                            dap(scr.rd, tl * 128,
                                [[S, 128], [128 * S, CH], [1, 128]])), 16)
                        if tl >= 1:
                            sT.wait(e, t0 + tl)
                            sO.inc(e.dma_start(
                                dap(self.out, (tl - 1) * 128 * D,
                                    [[D, 128], [1, D]]),
                                otok[(tl - 1) % 2][:, :]), 16)
                    sT.wait(e, t0 + 16)
                    sO.inc(e.dma_start(
                        dap(self.out, 15 * 128 * D, [[D, 128], [1, D]]),
                        otok[15 % 2][:, :]), 16)

                @blk.gpsimd
                def _(e):
                    e.wait_ge(s.dve.h, dve_pz + 2)
                    for tl in range(16):
                        if tl >= 2:
                            sSum.wait(e, su0 + tl - 1)
                        s.gp.inc(e.indirect_dma_start(
                            ytok[tl % 2][:, :], None,
                            dap(scr.ycomp, 0, [[D, 4352], [1, D]]),
                            IndirectOffsetOnAxis(t.sloti[:, tl:tl + 1], 0)),
                            16)

                @blk.tensor
                def _(e):
                    for tl in range(16):
                        sRl.wait(e, r0 + 16 * (tl + 1))
                        if tl >= 1:
                            sSum.wait(e, su0 + tl)
                        for c in range(CH):
                            ins = e.transpose(
                                pT[:, c * 128:(c + 1) * 128],
                                rfm[tl % 2][:, c * 128:(c + 1) * 128],
                                t.ident[:, :])
                        s.pe.inc(ins)

                @blk.vector
                def _(e):
                    gpb = s.gp.v - 16 * 16
                    act6 = s.act.v
                    for tl in range(16):
                        e.wait_ge(s.pe.h, pe0 + tl + 1)
                        e.wait_ge(s.gp.h, gpb + 16 * (tl + 1))
                        ins = e.tensor_tensor(xo[:, :], pT[:, :],
                                              ytok[tl % 2][:, :], ALU.add)
                        sSum.inc(ins)
                        sSum.wait(e, su0 + tl + 1)
                        s.dve.inc(e.reduce_sum(cols[:, 0:1], xo[:, :],
                                               axis=AX.X))
                        e.wait_ge(s.act.h, act6 + 3 * tl + 1)
                        s.dve.inc(e.reduce_sum(cols[:, 1:2], sqf[:, :],
                                               axis=AX.X))
                        e.wait_ge(s.dve.h, s.dve.v)
                        s.dve.inc(e.tensor_scalar_mul(cols[:, 0:1],
                                                      cols[:, 0:1], 1.0 / D))
                        e.wait_ge(s.dve.h, s.dve.v)
                        s.dve.inc(e.tensor_tensor(cols[:, 2:3], cols[:, 0:1],
                                                  cols[:, 0:1], ALU.mult))
                        s.dve.inc(e.tensor_scalar_mul(cols[:, 1:2],
                                                      cols[:, 1:2], 1.0 / D))
                        e.wait_ge(s.dve.h, s.dve.v)
                        ins = e.tensor_tensor(cols[:, 1:2], cols[:, 1:2],
                                              cols[:, 2:3], ALU.subtract)
                        sV2.inc(ins)
                        e.wait_ge(s.act.h, act6 + 3 * (tl + 1))
                        if tl >= 2:
                            sO.wait(e, o0 + 16 * (tl - 1))
                        s.dve.inc(e.tensor_tensor(cols[:, 2:3], cols[:, 0:1],
                                                  cols[:, 3:4], ALU.mult))
                        e.wait_ge(s.dve.h, s.dve.v)
                        s.dve.inc(e.tensor_scalar_mul(cols[:, 2:3],
                                                      cols[:, 2:3], -1.0))
                        e.wait_ge(s.dve.h, s.dve.v)
                        ins = e.tensor_scalar(
                            otok[tl % 2][:, :], xo[:, :], cols[:, 3:4],
                            cols[:, 2:3], ALU.mult, ALU.add)
                        sT.inc(ins)

                @blk.scalar
                def _(e):
                    for tl in range(16):
                        sSum.wait(e, su0 + tl + 1)
                        s.act.inc(e.activation(sqf[:, :], xo[:, :],
                                               AF.Square))
                        sV2.wait(e, v20 + tl + 1)
                        aa = s.act.v
                        s.act.inc(e.activation(cols[:, 3:4], cols[:, 1:2],
                                               AF.Ln, bias=t.epsc[:, 0:1]))
                        e.wait_ge(s.act.h, aa + 1)
                        s.act.inc(e.activation(cols[:, 3:4], cols[:, 3:4],
                                               AF.Exp, scale=-0.5))
            with nc.Block() as blk:
                @blk.sync
                def _(e):
                    sO.wait(e)
                    s.st.inc(e.nop())
        self.barrier()

    def dbg_qkv(self):
        self.dump("q", self.t.q_bf[:, :], [128, CH * S], cast=True)
        self.dump("k", self.t.k_bf[:, :], [128, CH * S], cast=True)
        self.dump("v", self.t.v_tok[:, :], [128, VW], cast=True)
        self.dump("negb", self.t.negb[:, :], [128, H])

    def dbg_ctx(self):
        self.dump("ctx", self.t.ctx_bf[:, :], [128, CH * S], cast=True)


def build_program(upto="all", dbg=()):
    return MK(upto=upto, dbg=dbg).build()


def kernel(**inputs):
    nc, _ = build_program()
    x = np.ascontiguousarray(inputs["L0_attn_x"])
    shared = dict(
        wq=np.ascontiguousarray(inputs["L0_attn_Wq"]),
        wk=np.ascontiguousarray(inputs["L0_attn_Wk"]),
        wv=np.ascontiguousarray(inputs["L0_attn_Wv"]),
        wo=np.ascontiguousarray(inputs["L0_attn_Wo"]),
        rel=np.ascontiguousarray(inputs["L0_attn_rel_emb"]),
        ln1w=np.ascontiguousarray(inputs["L0_ffn_ln1_w"]).reshape(1, D),
        ln1b=np.ascontiguousarray(inputs["L0_ffn_ln1_b"]).reshape(1, D),
        ln2w=np.ascontiguousarray(inputs["L0_ffn_ln2_w"]).reshape(1, D),
        ln2b=np.ascontiguousarray(inputs["L0_ffn_ln2_b"]).reshape(1, D),
        wqkv=np.ascontiguousarray(inputs["L0_ffn_w_qkv"]),
        wproj=np.ascontiguousarray(inputs["L0_ffn_w_proj"]),
        gatew=np.ascontiguousarray(inputs["L0_ffn_gate_w"]),
        gateb=np.ascontiguousarray(inputs["L0_ffn_gate_b"]).reshape(1, E),
        w1=np.ascontiguousarray(inputs["L0_ffn_w1"]),
        w2=np.ascontiguousarray(inputs["L0_ffn_w2"]),
    )
    in_maps = [dict(shared, x=np.ascontiguousarray(x[i])) for i in range(8)]
    res = run_bass_kernel_spmd(nc, in_maps, core_ids=list(range(8)))
    return np.stack([r["out"] for r in res.results], axis=0)
